# version 17
# baseline (speedup 1.0000x reference)
import math
import numpy as np
import ml_dtypes
from contextlib import ExitStack

import concourse.bass as bass
import concourse.mybir as mybir
from concourse.bass_utils import run_bass_kernel_spmd

F32 = mybir.dt.float32
F16 = mybir.dt.float16
BF16 = mybir.dt.bfloat16
AF = mybir.ActivationFunctionType
ALU = mybir.AluOpType
AX = mybir.AxisListType

S = 2048
E = 1024
H = 16
DK = 64
NKC = 8
NST = 16
NPAIR = 8
EPS = 1e-5
DEBUG = False


def _host_consts():
    theta = 10000.0 ** (np.arange(0, DK, 2, dtype=np.float64) / DK)
    pos = np.arange(S, dtype=np.float64)
    freqs = pos[:, None] * (1.0 / theta)[None, :]
    emb = np.concatenate([freqs, freqs], axis=-1)
    cos_t = np.cos(emb).T
    sin_t = np.sin(emb).T
    cos2 = np.concatenate([cos_t, cos_t], axis=0).astype(np.float16)
    sin2 = np.concatenate([sin_t, sin_t], axis=0).astype(np.float16)

    R64 = np.zeros((DK, DK), dtype=np.float64)
    for j in range(DK // 2):
        R64[2 * j, 2 * j + 1] = -1.0
        R64[2 * j + 1, 2 * j] = 1.0
    R2 = np.zeros((128, 128), dtype=np.float64)
    R2[0:64, 0:64] = R64
    R2[64:128, 64:128] = R64
    r2t = np.ascontiguousarray(R2.T).astype(np.float16)

    ident = np.eye(128, dtype=np.float16)
    kk, qq = np.meshgrid(np.arange(128), np.arange(128), indexing="ij")
    tri = (qq >= kk).astype(ml_dtypes.bfloat16)
    ones2 = np.zeros((128, 2), dtype=np.float16)
    ones2[0:64, 0] = 1.0
    ones2[64:128, 1] = 1.0
    ones_row = np.ones((1, 128), dtype=np.float16)
    return dict(cos2=cos2, sin2=sin2, r2t=r2t, ident=ident, tri=tri,
                ones2=ones2, ones_row=ones_row)


class Script:

    def __init__(self, ctr):
        self.ctr = ctr
        self.items = []

    def add(self, eng, fn, waits=(), inc=None):
        if inc is not None:
            name, n = inc
            self.ctr[name] = self.ctr.get(name, 0) + n
        self.items.append((eng, tuple(waits), fn, inc))
        return None if inc is None else self.ctr[inc[0]]

    def emit(self, blk, sems):
        by_eng = {}
        for eng, waits, fn, inc in self.items:
            by_eng.setdefault(eng, []).append((waits, fn, inc))
        for eng_name, items in by_eng.items():
            def mk(items):
                def _f(eng):
                    for waits, fn, inc in items:
                        for (sname, val) in waits:
                            eng.wait_ge(sems[sname], val)
                        inst = fn(eng)
                        if inc is not None:
                            assert inst is not None
                            inst.then_inc(sems[inc[0]], inc[1])
                return _f
            getattr(blk, eng_name)(mk(items))


def build_nc():
    nc = bass.Bass("TRN2", target_bir_lowering=False,
                   detect_race_conditions=False)

    d_x_tm = nc.dram_tensor("x_tm", [S, E], F32, kind="ExternalInput")
    d_x_fm = nc.dram_tensor("x_fm", [E, S], F32, kind="ExternalInput")
    d_wq = nc.dram_tensor("wq_t", [E, E], F32, kind="ExternalInput")
    d_wk = nc.dram_tensor("wk_t", [E, E], F32, kind="ExternalInput")
    d_wv = nc.dram_tensor("wv_t", [E, E], F32, kind="ExternalInput")
    d_wo = nc.dram_tensor("wo_t", [E, E], F32, kind="ExternalInput")
    d_wf = nc.dram_tensor("wf_t", [E, E], F32, kind="ExternalInput")
    d_d2 = nc.dram_tensor("disha2", [128, E], F32, kind="ExternalInput")
    d_bqr = nc.dram_tensor("bq_r", [128, 8], F32, kind="ExternalInput")
    d_bkr = nc.dram_tensor("bk_r", [128, 8], F32, kind="ExternalInput")
    d_bv = nc.dram_tensor("bv", [1, E], F32, kind="ExternalInput")
    d_bo = nc.dram_tensor("bo", [1, E], F32, kind="ExternalInput")
    d_cos = nc.dram_tensor("cos2", [128, S], F16, kind="ExternalInput")
    d_sin = nc.dram_tensor("sin2", [128, S], F16, kind="ExternalInput")
    d_r2t = nc.dram_tensor("r2t", [128, 128], F16, kind="ExternalInput")
    d_ident = nc.dram_tensor("identc", [128, 128], F16, kind="ExternalInput")
    d_tri = nc.dram_tensor("tric", [128, 128], BF16, kind="ExternalInput")
    d_ones2 = nc.dram_tensor("ones2c", [128, 2], F16, kind="ExternalInput")
    d_onesr = nc.dram_tensor("onesrc", [1, 128], F16, kind="ExternalInput")
    d_out = nc.dram_tensor("out", [S, E], F32, kind="ExternalOutput")
    if DEBUG:
        d_dbg = {name: nc.dram_tensor("dbg_" + name, shape, dt,
                                      kind="ExternalOutput")
                 for name, shape, dt in (
                     ("o_sb", [128, NST * E], F16),
                     ("q16", [128, S], F16),
                     ("k16", [128, S], F16),
                     ("qr16", [128, S], F16),
                     ("kr16", [128, S], F16),
                     ("v_pair", [128, NST * 130], F16),
                     ("negM", [128, 1], F32),
                     ("l4", [128, 8], F32),
                     ("oT16", [128, NKC * S], F16),
                     ("y1", [128, NST * E], F16),
                 )}

    ctr = {}
    est = ExitStack()
    with est:
        cos16 = est.enter_context(nc.sbuf_tensor("cos16", [128, S], F16))
        sin16 = est.enter_context(nc.sbuf_tensor("sin16", [128, S], F16))
        tri16 = est.enter_context(nc.sbuf_tensor("tri16", [128, 128], BF16))
        ident16 = est.enter_context(nc.sbuf_tensor("ident16", [128, 128], F16))
        r2t16 = est.enter_context(nc.sbuf_tensor("r2t16", [128, 128], F16))
        ones2_16 = est.enter_context(nc.sbuf_tensor("ones2_16", [128, 2], F16))
        onesr16 = est.enter_context(nc.sbuf_tensor("onesr16", [1, 128], F16))
        bq_sb = est.enter_context(nc.sbuf_tensor("bq_sb", [128, 8], F32))
        bk_sb = est.enter_context(nc.sbuf_tensor("bk_sb", [128, 8], F32))
        bv16 = est.enter_context(nc.sbuf_tensor("bv16", [1, E], F16))
        bo16 = est.enter_context(nc.sbuf_tensor("bo16", [1, E], F16))
        x_tm16 = est.enter_context(nc.sbuf_tensor("x_tm16", [128, NST * E], F16))
        o_sb = est.enter_context(nc.sbuf_tensor("o_sb", [128, NST * E], F16))
        negM2 = est.enter_context(nc.sbuf_tensor("negM2", [128, 2], F32))
        nrm = est.enter_context(nc.sbuf_tensor("nrm", [128, 2], F32))
        ab16 = est.enter_context(nc.sbuf_tensor("ab16", [128, 1], F16))
        scr32 = est.enter_context(nc.sbuf_tensor("scr32", [32, 32], F32))
        scr32t = est.enter_context(nc.sbuf_tensor("scr32t", [32, 32], F32))
        mm1 = est.enter_context(nc.sbuf_tensor("mm1", [1, 1], F32))
        negm16 = est.enter_context(nc.sbuf_tensor("negm16", [1, 1], F16))
        l4 = est.enter_context(nc.sbuf_tensor("l4", [128, 8], F32))
        rl4 = est.enter_context(nc.sbuf_tensor("rl4", [128, 8], F32))
        bnst = est.enter_context(nc.sbuf_tensor("bnst", [128, 12], F32))
        mv = est.enter_context(nc.sbuf_tensor("mv", [128, 2], F32))
        lnv = est.enter_context(nc.sbuf_tensor("lnv", [128, 1], F32))
        rstd = est.enter_context(nc.sbuf_tensor("rstd", [128, 1], F32))
        epsb = est.enter_context(nc.sbuf_tensor("epsb", [128, 1], F32))

        ps0 = est.enter_context(nc.psum_tensor("ps0", [128, 1024], F32))
        ps1 = est.enter_context(nc.psum_tensor("ps1", [128, 1024], F32))
        pavA = est.enter_context(nc.psum_tensor("pavA", [128, 260], F32))
        pavB = est.enter_context(nc.psum_tensor("pavB", [128, 260], F32))

        sems = {}
        for name in ("te", "act", "dve", "dG", "dS", "swA", "swB", "sxm",
                     "swo", "wf", "oA", "oB", "gp"):
            sems[name] = est.enter_context(nc.semaphore("sem_" + name))

        def te(n):
            return ("te", n)

        def bump(name, n=1):
            ctr[name] = ctr.get(name, 0) + n
            return ctr[name]

        def cur(name):
            return ctr.get(name, 0)

        pair_stack = ExitStack()
        with pair_stack:
            xT16 = est.enter_context(
                nc.sbuf_tensor("xT16", [128, NKC * S], F16))
            wq16 = [pair_stack.enter_context(
                nc.sbuf_tensor(f"wq16_{i}", [128, 1024], F16)) for i in (0, 1)]
            wk16 = [pair_stack.enter_context(
                nc.sbuf_tensor(f"wk16_{i}", [128, 1024], F16)) for i in (0, 1)]
            wv16 = pair_stack.enter_context(
                nc.sbuf_tensor("wv16", [128, NKC * 512], F16))
            q16 = pair_stack.enter_context(nc.sbuf_tensor("q16", [128, S], F16))
            k16 = pair_stack.enter_context(nc.sbuf_tensor("k16", [128, S], F16))
            rqs = pair_stack.enter_context(nc.sbuf_tensor("rqs", [128, S], F16))
            qr16d = [pair_stack.enter_context(
                nc.sbuf_tensor(f"qr16_{i}", [128, S], F16)) for i in (0, 1)]
            kr16d = [pair_stack.enter_context(
                nc.sbuf_tensor(f"kr16_{i}", [128, S], F16)) for i in (0, 1)]
            v_quadd = [pair_stack.enter_context(
                nc.sbuf_tensor(f"v_quad_{i}", [128, NST * 520], F16))
                for i in (0, 1)]
            NPT = 6
            pT = [pair_stack.enter_context(
                nc.sbuf_tensor(f"pT{i}", [128, 1024], BF16))
                for i in range(NPT)]
            scrA = pair_stack.enter_context(
                nc.sbuf_tensor("scrA", [128, 260], F32))
            scrB = pair_stack.enter_context(
                nc.sbuf_tensor("scrB", [128, 260], F32))
            psq_es = ExitStack()
            psq0 = psq_es.enter_context(
                nc.psum_tensor("psq0", [128, 512], F32))
            psq1 = psq_es.enter_context(
                nc.psum_tensor("psq1", [128, 512], F32))
            ptr_holder = [None]
            oT16_holder = [None]
            m_ot_ev = {}
            m_div_qb = {}

            def wslice_dma(g, dst, dram, c):
                src = bass.AP(dram, c * 128,
                              [[1024, 128], [128 * 1024, 8], [1, 128]])
                dstap = bass.AP(dst, 0, [[1024, 128], [128, 8], [1, 128]])
                return g.dma_start(dstap, src)

            def wvslice_dma(g, dst, grp):
                src = bass.AP(d_wv, grp * 512,
                              [[1024, 128], [128 * 1024, 8], [1, 512]])
                dstap = bass.AP(dst, 0, [[NKC * 512, 128], [512, 8], [1, 512]])
                return g.dma_start(dstap, src)

            sc = Script(ctr)
            for dstt, dram in ((cos16, d_cos), (sin16, d_sin), (tri16, d_tri),
                               (ident16, d_ident), (r2t16, d_r2t),
                               (ones2_16, d_ones2), (onesr16, d_onesr),
                               (bq_sb, d_bqr), (bk_sb, d_bkr)):
                sc.add("sync", lambda s, a=dstt, b=dram: s.dma_start(
                    a[:, :] if a.shape[0] > 1 else a[0:1, :], b.ap()),
                    inc=("dS", 16))
            for kc in range(NKC):
                sc.add("gpsimd", lambda g, kc=kc: g.dma_start(
                    xT16[:, kc * S:(kc + 1) * S],
                    d_x_fm.ap()[kc * 128:(kc + 1) * 128, :]), inc=("dG", 16))
            xT_ready = cur("dG")
            sc.add("gpsimd", lambda g: g.dma_start(
                bv16[0:1, :], d_bv.ap()), inc=("dG", 16))
            sc.add("gpsimd", lambda g: g.dma_start(
                bo16[0:1, :], d_bo.ap()), inc=("dG", 16))
            for (ring, dram) in ((wq16, d_wq), (wk16, d_wk)):
                sc.add("gpsimd", lambda g, r=ring, d=dram: wslice_dma(
                    g, r[0], d, 0), inc=("swA", 16))
            sc.add("gpsimd", lambda g: wvslice_dma(g, wv16, 0),
                   inc=("swA", 16))
            sc.add("vector", lambda v: v.memset(scr32[:, :], 0.0))
            sc.add("vector", lambda v: v.memset(epsb[:, :], EPS))
            sc.add("sync", lambda s: s.nop(), waits=[("dS", cur("dS"))])
            sc.add("gpsimd", lambda g: g.nop(),
                   waits=[("dG", cur("dG")), ("swA", cur("swA"))])
            with nc.Block() as blk:
                sc.emit(blk, sems)

            m_expfree = {0: ("dve", 0), 1: ("dve", 0)}
            pshalf = [ps0, ps1]
            pav = [pavA, pavB]
            psq = [psq0, psq1]
            pairinfo = {}
            m_evac_prev = [None]

            for stage in range(NPAIR + 1):
                sc = Script(ctr)
                cq = stage if stage < NPAIR else None
                ca = stage - 1 if stage >= 1 else None
                if stage == NPAIR:
                    psq_es.close()
                    ptr_holder[0] = [est.enter_context(
                        nc.psum_tensor(f"ptr{i}", [128, 512], F16))
                        for i in (0, 1)]
                    oT16_holder[0] = xT16

                def ot_transposes(qlo, qhi, gate_qb):
                    ptr = ptr_holder[0]
                    oT16 = oT16_holder[0]
                    for g in range(qlo * 2, qhi * 2):
                        qt, oc4 = divmod(g, 2)
                        pslot = g % 2
                        waits = [("dve", m_div_qb[gate_qb])]
                        if (g - 2) in m_ot_ev:
                            waits.append(("dve", m_ot_ev[g - 2]))

                        def trf(t, qt=qt, oc4=oc4, pslot=pslot, ptr=ptr):
                            for u in range(4):
                                oc = oc4 * 4 + u
                                inst = t.transpose(
                                    ptr[pslot][:, u * 128:(u + 1) * 128],
                                    o_sb[:, qt * E + oc * 128:
                                         qt * E + oc * 128 + 128],
                                    ident16[:, :])
                            return inst
                        m_ps = sc.add("tensor", trf, waits=waits,
                                      inc=("te", 1))
                        m_ot_ev[g] = sc.add(
                            "vector",
                            lambda v, qt=qt, oc4=oc4, pslot=pslot, ptr=ptr:
                            v.tensor_copy(
                                bass.AP(oT16, (oc4 * 4) * S + qt * 128,
                                        [[NKC * S, 128], [S, 4], [1, 128]]),
                                bass.AP(ptr[pslot], 0,
                                        [[512, 128], [128, 4], [1, 128]])),
                            waits=[("te", m_ps)], inc=("dve", 1))

                if cq is not None:
                    if cq + 1 < NPAIR:
                        wsem = ("swA", "swB")[(cq + 1) % 2]
                        for (ring, dram) in ((wq16, d_wq), (wk16, d_wk)):
                            sc.add("gpsimd",
                                   lambda g, r=ring, d=dram, cc=cq + 1:
                                   wslice_dma(g, r[cc % 2], d, cc),
                                   inc=(wsem, 16))
                        if cq + 1 == 4:
                            sc.add("gpsimd", lambda g: wvslice_dma(
                                g, wv16, 1), inc=(wsem, 16))
                    if cq == 3:
                        for st in range(NST):
                            sc.add("gpsimd", lambda g, st=st: g.dma_start(
                                x_tm16[:, st * E:(st + 1) * E],
                                d_x_tm.ap()[st * 128:(st + 1) * 128, :]),
                                inc=("sxm", 16))

                if cq is not None:
                    c = cq
                    slot = c % 2
                    qr16 = qr16d[c % 2]
                    kr16 = kr16d[c % 2]
                    v_quad = v_quadd[(c // 4) % 2]
                    last_ev = [("dve", 0), ("dve", 0)]
                    info = {}
                    pairinfo[c] = info

                    def proj_unit(kind, i, c=c, slot=slot, info=info,
                                  last_ev=last_ev):
                        bank = i % 2
                        waits = []
                        if last_ev[bank][1] > 0:
                            waits.append(last_ev[bank])
                        if kind == "q" and i == 0 and c >= 1:
                            if c % 2 == 0:
                                nw = 48 + 32 * (c // 2) + (16 if c >= 6
                                                           else 0)
                            else:
                                nw = 32 * ((c + 1) // 2)
                            waits.append((("swA", "swB")[c % 2], nw))
                        if kind in ("rq",) and i == 0:
                            waits.append(("act", info["q_ev3"]))
                        if kind in ("rk",) and i == 0:
                            waits.append(("act", info["k_ev3"]))

                        if kind in ("q", "k"):
                            w16 = wq16 if kind == "q" else wk16

                            def mmf(t, i=i, bank=bank, slot=slot, w16=w16):
                                for kc in range(NKC):
                                    inst = t.matmul(
                                        psq[bank][:, :],
                                        w16[slot][:, kc * 128:(kc + 1) * 128],
                                        xT16[:, kc * S + i * 512:
                                             kc * S + i * 512 + 512],
                                        start=(kc == 0), stop=(kc == NKC - 1))
                                return inst
                        else:
                            src16 = q16 if kind == "rq" else k16

                            def mmf(t, i=i, bank=bank, src16=src16):
                                return t.matmul(
                                    psq[bank][:, :], r2t16[:, :],
                                    src16[:, i * 512:(i + 1) * 512],
                                    start=True, stop=True)
                        m_ps = sc.add("tensor", mmf, waits=waits,
                                      inc=("te", 1))
                        if kind in ("q", "k"):
                            dst = q16 if kind == "q" else k16
                            bias = bq_sb if kind == "q" else bk_sb
                            m_ev = sc.add(
                                "scalar",
                                lambda s, i=i, bank=bank, dst=dst, bias=bias,
                                c=c: s.activation(
                                    dst[:, i * 512:(i + 1) * 512],
                                    psq[bank][:, :], AF.Identity,
                                    bias=bias[:, c:c + 1], scale=1.0),
                                waits=[("te", m_ps)], inc=("act", 1))
                            last_ev[bank] = ("act", m_ev)
                            if i == 3:
                                info[kind + "_ev3"] = m_ev
                        else:
                            dst = rqs
                            m_ev = sc.add(
                                "vector",
                                lambda v, i=i, bank=bank, dst=dst:
                                v.tensor_copy(
                                    dst[:, i * 512:(i + 1) * 512],
                                    psq[bank][:, :]),
                                waits=[("te", m_ps)], inc=("dve", 1))
                            last_ev[bank] = ("dve", m_ev)
                            if i == 3:
                                info[kind + "_ev3"] = m_ev

                    def rope_unit(kind, c=c, info=info, qr16=qr16,
                                  kr16=kr16):
                        src16 = q16 if kind == "q" else k16
                        dst = qr16 if kind == "q" else kr16
                        ev3 = info[("q" if kind == "q" else "k") + "_ev3"]
                        sc.add("vector", lambda v, s16=src16, dst=dst:
                               v.tensor_mul(
                                   dst[:, :], s16[:, :], cos16[:, :]),
                               waits=[("act", ev3)])
                        sc.add("vector", lambda v: v.tensor_mul(
                            rqs[:, :], rqs[:, :], sin16[:, :]))
                        info["rope_" + kind] = sc.add(
                            "vector", lambda v, dst=dst: v.tensor_add(
                                dst[:, :], dst[:, :], rqs[:, :]),
                            inc=("dve", 1))

                    def norms_unit(info=info, qr16=qr16, kr16=kr16):
                        sc.add("vector", lambda v: v.reduce_max(
                            nrm[:, 0:1], qr16[:, :], axis=AX.X,
                            apply_absolute_value=True))
                        m_nrm = sc.add("vector", lambda v: v.reduce_max(
                            nrm[:, 1:2], kr16[:, :], axis=AX.X,
                            apply_absolute_value=True), inc=("dve", 1))
                        info["ab"] = sc.add(
                            "vector", lambda v: v.tensor_mul(
                                ab16[:, 0:1], nrm[:, 0:1], nrm[:, 1:2]),
                            waits=[("dve", m_nrm)], inc=("dve", 1))

                    def v_units(lo, hi, c=c, info=info, last_ev=last_ev,
                                v_quad=v_quad):
                        grp = c // 4
                        if lo == 0:
                            sc.add("vector", lambda v: v.memset(
                                v_quad[:, :], 1.0))
                        for st in range(lo, hi):
                            bank = st % 2
                            waits = []
                            if last_ev[bank][1] > 0:
                                waits.append(last_ev[bank])

                            def vmm(t, st=st, bank=bank, grp=grp,
                                    slot=grp % 2):
                                for kc in range(NKC):
                                    t.matmul(
                                        psq[bank][:, :],
                                        xT16[:, kc * S + st * 128:
                                             kc * S + st * 128 + 128],
                                        wv16[:, kc * 512:(kc + 1) * 512],
                                        start=(kc == 0), stop=False)
                                return t.matmul(
                                    psq[bank][:, :], onesr16[0:1, :],
                                    bv16[0:1, grp * 512:(grp + 1) * 512],
                                    start=False, stop=True)
                            m_ps = sc.add("tensor", vmm, waits=waits,
                                          inc=("te", 1))
                            m_ev = sc.add(
                                "vector",
                                lambda v, st=st, bank=bank, v_quad=v_quad:
                                v.tensor_copy(
                                    bass.AP(v_quad, st * 520,
                                            [[NST * 520, 128], [130, 4],
                                             [65, 2], [1, 64]]),
                                    bass.AP(psq[bank], 0,
                                            [[512, 128], [128, 4],
                                             [64, 2], [1, 64]])),
                                waits=[("te", m_ps)], inc=("dve", 1))
                            last_ev[bank] = ("dve", m_ev)
                        if hi == NST:
                            info["vall"] = last_ev[0][1] if \
                                last_ev[0][0] == "dve" else None
                            info["vall"] = max(
                                x[1] for x in last_ev if x[0] == "dve")

                    def mchain_unit(c=c, info=info, last_ev=last_ev):
                        par = c % 2
                        waits = [("dve", info["ab"])]
                        if last_ev[0][1] > 0:
                            waits.append(last_ev[0])
                        m_pn = sc.add("tensor", lambda t: t.matmul(
                            psq0[0:2, 448:449], ones2_16[:, :], ab16[:, 0:1],
                            start=True, stop=True, skip_group_check=True),
                            waits=waits, inc=("te", 1))
                        m_c1 = sc.add("vector", lambda v: v.tensor_copy(
                            scr32[0:2, 0:1], psq0[0:2, 448:449]),
                            waits=[("te", m_pn)], inc=("dve", 1))
                        m_c2 = sc.add("vector", lambda v: v.transpose(
                            scr32t[:, :], scr32[:, :]),
                            waits=[("dve", m_c1)], inc=("dve", 1))
                        m_c3 = sc.add("vector", lambda v: v.reduce_max(
                            mm1[0:1, 0:1], scr32t[0:1, 0:2], axis=AX.X),
                            waits=[("dve", m_c2)], inc=("dve", 1))
                        m_n16 = sc.add("vector", lambda v: v.tensor_scalar_mul(
                            negm16[0:1, 0:1], mm1[0:1, 0:1], -1.02 / 8.0),
                            waits=[("dve", m_c3)], inc=("dve", 1))
                        waits = [("dve", m_n16)]
                        if last_ev[1][1] > 0:
                            waits.append(last_ev[1])
                        m_pbc = sc.add("tensor", lambda t: t.matmul(
                            psq1[:, 448:449], onesr16[0:1, :],
                            negm16[0:1, 0:1],
                            start=True, stop=True, skip_group_check=True),
                            waits=waits, inc=("te", 1))
                        info["negM"] = sc.add(
                            "vector", lambda v, par=par: v.tensor_copy(
                                negM2[:, par:par + 1], psq1[:, 448:449]),
                            waits=[("te", m_pbc)], inc=("dve", 1))

                if ca is not None:
                    cc = ca
                    ainfo = pairinfo[cc]
                    a_qr = qr16d[cc % 2]
                    a_kr = kr16d[cc % 2]
                    a_vq = v_quadd[(cc // 4) % 2]
                    avpar = {0: None, 1: None}
                    avslot = {}

                    def att_qb(qb, cc=cc, ainfo=ainfo, a_qr=a_qr,
                               a_kr=a_kr, a_vq=a_vq, avpar=avpar):
                        ktmax = 4 * (qb + 1)
                        st_meta = {}

                        def emit_sT_mm(kt):
                            par = kt % 2
                            ps = pshalf[par]
                            qlo = max(qb * 512, kt * 128)
                            W = (qb + 1) * 512 - qlo
                            rel = qlo - qb * 512
                            waits = []
                            prev = m_expfree[par]
                            if prev[1] > 0:
                                waits.append(("act", prev[1]))

                            def smm(t, ps=ps, rel=rel, W=W, kt=kt, qb=qb):
                                t.matmul(ps[:, rel:rel + W],
                                         a_kr[0:64, kt * 128:kt * 128 + 128],
                                         a_qr[0:64, qb * 512 + rel:
                                              qb * 512 + rel + W],
                                         start=True, stop=True,
                                         tile_position=(0, 0))
                                return t.matmul(
                                    ps[:, 512 + rel:512 + rel + W],
                                    a_kr[64:128, kt * 128:kt * 128 + 128],
                                    a_qr[64:128, qb * 512 + rel:
                                         qb * 512 + rel + W],
                                    start=True, stop=True,
                                    tile_position=(64, 0))
                            m_sT = sc.add("tensor", smm, waits=waits,
                                          inc=("te", 1))
                            st_meta[kt] = [m_sT, None, None, rel]

                        def emit_expmask(kt):
                            par = kt % 2
                            slot = kt % NPT
                            pt = pT[slot]
                            m_sT, _, _, rel = st_meta[kt]
                            W = (qb + 1) * 512 - (rel + qb * 512)
                            waits = [("te", m_sT)]
                            if avslot.get(slot) is not None:
                                waits.append(("te", avslot[slot]))

                            def expf(s, pt=pt, par=par, rel=rel, W=W,
                                     cc=cc):
                                return s.activation(
                                    bass.AP(pt, rel, [[1024, 128], [512, 2],
                                                      [1, W]]),
                                    bass.AP(pshalf[par], rel,
                                            [[1024, 128], [512, 2], [1, W]]),
                                    AF.Exp,
                                    bias=negM2[:, cc % 2:cc % 2 + 1],
                                    scale=0.125)
                            m_exp = sc.add("scalar", expf, waits=waits,
                                           inc=("act", 1))
                            m_expfree[par] = ("act", m_exp)
                            m_mask = None
                            if kt >= 4 * qb:
                                def maskf(v, pt=pt, rel=rel):
                                    return v.tensor_tensor(
                                        bass.AP(pt, rel,
                                                [[1024, 128], [512, 2],
                                                 [1, 128]]),
                                        bass.AP(pt, rel,
                                                [[1024, 128], [512, 2],
                                                 [1, 128]]),
                                        bass.AP(tri16, 0,
                                                [[128, 128], [0, 2],
                                                 [1, 128]]),
                                        ALU.mult)
                                m_mask = sc.add("vector", maskf,
                                                waits=[("act", m_exp)],
                                                inc=("dve", 1))
                            st_meta[kt][1] = m_exp
                            st_meta[kt][2] = m_mask

                        def emit_av(kt):
                            par = kt % 2
                            slot = kt % NPT
                            pt = pT[slot]
                            _, m_exp, m_mask, rel = st_meta[kt]
                            j0 = rel // 128
                            waits = [("act", m_exp)]
                            if m_mask is not None:
                                waits.append(("dve", m_mask))
                            if kt == 0:
                                waits.append(("dve", ainfo["vall"]))
                                if m_evac_prev[0] is not None:
                                    waits.append(("dve", m_evac_prev[0]))

                            def avf(t, pt=pt, kt=kt, qb=qb, j0=j0,
                                    ktmax=ktmax, cc=cc):
                                for head in range(2):
                                    for j in range(j0, 4):
                                        base = head * 512 + j * 128
                                        inst = t.matmul(
                                            pav[head][:, j * 65:j * 65 + 65],
                                            pt[:, base:base + 128],
                                            a_vq[:, kt * 520 +
                                                 (cc % 4) * 130 + head * 65:
                                                 kt * 520 + (cc % 4) * 130 +
                                                 head * 65 + 65],
                                            start=(kt == 0 and j == 0),
                                            stop=(kt == ktmax - 1 and
                                                  j == 3),
                                            skip_group_check=True)
                                return inst
                            m_av = sc.add("tensor", avf, waits=waits,
                                          inc=("te", 1))
                            avpar[par] = m_av
                            avslot[slot] = m_av
                            return m_av

                        m_av_last = None
                        for g0 in range(0, ktmax, 4):
                            for kt in range(g0, min(g0 + 4, ktmax)):
                                emit_sT_mm(kt)
                                emit_expmask(kt)
                            for kt in range(g0, min(g0 + 4, ktmax)):
                                m_av_last = emit_av(kt)

                        sc.add("vector", lambda v: v.tensor_copy(
                            scrA[:, :], pavA[:, :]),
                            waits=[("te", m_av_last)])
                        m_ev2 = sc.add("vector", lambda v: v.tensor_copy(
                            scrB[:, :], pavB[:, :]), inc=("dve", 1))
                        m_evac_prev[0] = m_ev2
                        sc.add("vector", lambda v: v.tensor_copy(
                            l4[:, 0:4],
                            bass.AP(scrA, 64, [[260, 128], [65, 4]])),
                            waits=[("dve", m_ev2)])
                        m_lcp = sc.add("vector", lambda v: v.tensor_copy(
                            l4[:, 4:8],
                            bass.AP(scrB, 64, [[260, 128], [65, 4]])),
                            inc=("dve", 1))
                        m_rcp = sc.add("vector", lambda v: v.reciprocal(
                            rl4[:, 0:8], l4[:, 0:8]),
                            waits=[("dve", m_lcp)], inc=("dve", 1))
                        scr = [scrA, scrB]
                        for head in range(2):
                            for j in range(4):
                                qt = 4 * qb + j
                                hh = 2 * cc + head

                                def divf(v, head=head, j=j, qt=qt, hh=hh):
                                    return v.tensor_scalar_mul(
                                        o_sb[:, qt * E + hh * 64:
                                             qt * E + hh * 64 + 64],
                                        scr[head][:, j * 65:j * 65 + 64],
                                        rl4[:, head * 4 + j:
                                            head * 4 + j + 1])
                                m_dv = sc.add(
                                    "vector", divf,
                                    waits=[("dve", m_rcp)] if
                                    (head == 0 and j == 0) else [],
                                    inc=("dve", 1) if
                                    (head == 1 and j == 3) else None)
                        m_div_qb[qb] = m_dv

                if cq is not None:
                    proj_unit("q", 0)
                    proj_unit("q", 1)
                if ca is not None:
                    att_qb(0)
                    if ca == NPAIR - 1:
                        ot_transposes(0, 4, 0)
                if cq is not None:
                    proj_unit("q", 2)
                    proj_unit("q", 3)
                    for i in range(4):
                        proj_unit("rq", i)
                    rope_unit("q")
                if ca is not None:
                    att_qb(1)
                    if ca == NPAIR - 1:
                        ot_transposes(4, 8, 1)
                if cq is not None:
                    for i in range(4):
                        proj_unit("k", i)
                    for i in range(4):
                        proj_unit("rk", i)
                    rope_unit("k")
                    norms_unit()
                if ca is not None:
                    att_qb(2)
                    if ca == NPAIR - 1:
                        ot_transposes(8, 12, 2)
                if cq is not None and cq % 4 == 0:
                    v_units(0, 8)
                if ca is not None:
                    att_qb(3)
                    if ca == NPAIR - 1:
                        ot_transposes(12, 16, 3)
                if cq is not None:
                    if cq % 4 == 0:
                        v_units(8, NST)
                    else:
                        pairinfo[cq]["vall"] = pairinfo[cq - 1]["vall"]
                    mchain_unit()

                with nc.Block() as blk:
                    sc.emit(blk, sems)

        post_stack = ExitStack()
        with post_stack:
            oT16 = oT16_holder[0]
            woT16 = post_stack.enter_context(
                nc.sbuf_tensor("woT16", [128, NKC * E], F16))
            y1T16 = post_stack.enter_context(
                nc.sbuf_tensor("y1T16", [128, NKC * S], F16))
            y2r = post_stack.enter_context(
                nc.sbuf_tensor("y2r", [128, 2 * E], F32))
            weff16 = post_stack.enter_context(
                nc.sbuf_tensor("weff16", [128, NKC * E], F16))
            wff32 = post_stack.enter_context(
                nc.sbuf_tensor("wff32", [128, E], F32))
            disha2 = post_stack.enter_context(
                nc.sbuf_tensor("disha2s", [128, E], F32))
            ptr = ptr_holder[0]

            sc = Script(ctr)
            m_d2 = sc.add("sync", lambda s: s.dma_start(
                disha2[:, :], d_d2.ap()), inc=("dS", 16))
            for oc in range(NKC):
                sc.add("gpsimd", lambda g, oc=oc: g.dma_start(
                    woT16[:, oc * E:(oc + 1) * E],
                    d_wo.ap()[oc * 128:(oc + 1) * 128, :]),
                    inc=("swo", 16))
            m_weff = None
            for fc in range(NKC):
                waits = []
                if fc >= 1:
                    waits.append(("gp", m_weff))
                m_wdma = sc.add("sync", lambda s, fc=fc: s.dma_start(
                    wff32[:, :],
                    d_wf.ap()[fc * 128:(fc + 1) * 128, :]),
                    waits=waits, inc=("wf", 16))
                m_weff = sc.add("gpsimd", lambda g, fc=fc: g.tensor_add(
                    weff16[:, fc * E:(fc + 1) * E],
                    wff32[:, :],
                    disha2[:, :]),
                    waits=[("wf", m_wdma)] + ([("dS", m_d2)] if fc == 0
                                              else []),
                    inc=("gp", 1))
            m_z1 = {}
            for st in range(NST):
                ps = [ps0, ps1][st % 2]
                waits = [("dve", m_ot_ev[st * 2 + 1])]
                if st == 0:
                    waits.append(("swo", 128))
                    waits.append(("sxm", 256))
                if st >= 2:
                    waits.append(("dve", m_z1[st - 2]))

                def aomm(t, ps=ps, st=st):
                    for oc in range(NKC):
                        t.matmul(ps[:, 0:512],
                                 oT16[:, oc * S + st * 128:
                                      oc * S + st * 128 + 128],
                                 woT16[:, oc * E:oc * E + 512],
                                 start=(oc == 0), stop=False)
                        t.matmul(ps[:, 512:1024],
                                 oT16[:, oc * S + st * 128:
                                      oc * S + st * 128 + 128],
                                 woT16[:, oc * E + 512:oc * E + 1024],
                                 start=(oc == 0), stop=False)
                    t.matmul(ps[:, 0:512], onesr16[0:1, :],
                             bo16[0:1, 0:512], start=False, stop=True)
                    return t.matmul(ps[:, 512:1024], onesr16[0:1, :],
                                    bo16[0:1, 512:1024], start=False,
                                    stop=True)
                m_ao = sc.add("tensor", aomm, waits=waits, inc=("te", 1))

                m_z1[st] = sc.add("vector", lambda v, ps=ps, st=st:
                                  v.tensor_add(
                                      x_tm16[:, st * E:(st + 1) * E],
                                      x_tm16[:, st * E:(st + 1) * E],
                                      ps[:, :]),
                                  waits=[("te", m_ao)], inc=("dve", 1))
                sc.add("vector", lambda v, st=st: v.bn_stats(
                    bnst[:, 0:6], x_tm16[:, st * E:st * E + 512]))
                m_bs = sc.add("vector", lambda v, st=st: v.bn_stats(
                    bnst[:, 6:12], x_tm16[:, st * E + 512:(st + 1) * E]),
                    inc=("dve", 1))
                m_bn = sc.add("vector", lambda v: v.bn_aggr(
                    mv[:, 0:2], bnst[:, 0:12]), waits=[("dve", m_bs)],
                    inc=("dve", 1))
                m_ln = sc.add("scalar", lambda s: s.activation(
                    lnv[:, 0:1], mv[:, 1:2], AF.Ln, bias=epsb[:, 0:1], scale=1.0),
                    waits=[("dve", m_bn)], inc=("act", 1))
                m_rstd = sc.add("scalar", lambda s: s.activation(
                    rstd[:, 0:1], lnv[:, 0:1], AF.Exp, scale=-0.5),
                    waits=[("act", m_ln)], inc=("act", 1))
                sc.add("vector", lambda v, st=st: v.tensor_scalar(
                    x_tm16[:, st * E:(st + 1) * E],
                    x_tm16[:, st * E:(st + 1) * E],
                    mv[:, 0:1], rstd[:, 0:1],
                    ALU.subtract, ALU.mult),
                    waits=[("act", m_rstd)], inc=("dve", 1))
            ln1_done = cur("dve")
            with nc.Block() as blk:
                sc.emit(blk, sems)

            sc = Script(ctr)
            m_yt_ev = []
            for g in range(32):
                qt, fc4 = divmod(g, 2)
                slot = g % 2
                waits = []
                if g >= 2:
                    waits.append(("dve", m_yt_ev[g - 2]))

                def trf(t, qt=qt, fc4=fc4, slot=slot):
                    for u in range(4):
                        fc = fc4 * 4 + u
                        inst = t.transpose(
                            ptr[slot][:, u * 128:(u + 1) * 128],
                            x_tm16[:, qt * E + fc * 128:
                                   qt * E + fc * 128 + 128],
                            ident16[:, :])
                    return inst
                m_ps = sc.add("tensor", trf, waits=waits, inc=("te", 1))
                m_yt_ev.append(sc.add(
                    "vector",
                    lambda v, qt=qt, fc4=fc4, slot=slot: v.tensor_copy(
                        bass.AP(y1T16, (fc4 * 4) * S + qt * 128,
                                [[NKC * S, 128], [S, 4], [1, 128]]),
                        bass.AP(ptr[slot], 0, [[512, 128], [128, 4], [1, 128]])),
                    waits=[("te", m_ps)], inc=("dve", 1)))

            if DEBUG:
                for name, t in (("o_sb", o_sb), ("oT16", oT16),
                                ("y1", x_tm16)):
                    sc.add("sync", lambda s, name=name, t=t: s.dma_start(
                        d_dbg[name].ap(), t[:, :]), inc=("dS", 16))
                sc.add("sync", lambda s: s.nop(), waits=[("dS", cur("dS"))])
            m_y2 = {}
            m_outdma = {}
            for st in range(NST):
                ps = [ps0, ps1][st % 2]
                waits = [("dve", m_yt_ev[st * 2 + 1])]
                if st == 0:
                    waits.append(("gp", m_weff))
                if st >= 2:
                    waits.append(("dve", m_y2[st - 2]))

                def ffmm(t, ps=ps, st=st):
                    for fc in range(NKC):
                        t.matmul(ps[:, 0:512],
                                 y1T16[:, fc * S + st * 128:
                                       fc * S + st * 128 + 128],
                                 weff16[:, fc * E:fc * E + 512],
                                 start=(fc == 0), stop=(fc == NKC - 1))
                        inst = t.matmul(ps[:, 512:1024],
                                        y1T16[:, fc * S + st * 128:
                                              fc * S + st * 128 + 128],
                                        weff16[:, fc * E + 512:fc * E + 1024],
                                        start=(fc == 0), stop=(fc == NKC - 1))
                    return inst
                m_ff = sc.add("tensor", ffmm, waits=waits, inc=("te", 1))

                yslot = st % 2
                zwaits = [("te", m_ff)]
                if st >= 2:
                    zwaits.append((("oA", "oB")[yslot], m_outdma[st - 2]))
                sc.add("vector", lambda v, ps=ps, st=st, yslot=yslot:
                       v.tensor_add(
                           y2r[:, yslot * E:(yslot + 1) * E],
                           x_tm16[:, st * E:(st + 1) * E],
                           ps[:, :]),
                       waits=zwaits)
                sc.add("vector", lambda v, yslot=yslot: v.bn_stats(
                    bnst[:, 0:6], y2r[:, yslot * E:yslot * E + 512]))
                m_bs2 = sc.add("vector", lambda v, yslot=yslot: v.bn_stats(
                    bnst[:, 6:12], y2r[:, yslot * E + 512:(yslot + 1) * E]),
                    inc=("dve", 1))
                m_bn = sc.add("vector", lambda v: v.bn_aggr(
                    mv[:, 0:2], bnst[:, 0:12]), waits=[("dve", m_bs2)],
                    inc=("dve", 1))
                m_ln = sc.add("scalar", lambda s: s.activation(
                    lnv[:, 0:1], mv[:, 1:2], AF.Ln, bias=epsb[:, 0:1], scale=1.0),
                    waits=[("dve", m_bn)], inc=("act", 1))
                m_rstd = sc.add("scalar", lambda s: s.activation(
                    rstd[:, 0:1], lnv[:, 0:1], AF.Exp, scale=-0.5),
                    waits=[("act", m_ln)], inc=("act", 1))
                m_y2[st] = sc.add("vector", lambda v, yslot=yslot:
                                  v.tensor_scalar(
                                      y2r[:, yslot * E:(yslot + 1) * E],
                                      y2r[:, yslot * E:(yslot + 1) * E],
                                      mv[:, 0:1], rstd[:, 0:1],
                                      ALU.subtract, ALU.mult),
                                  waits=[("act", m_rstd)], inc=("dve", 1))
                m_outdma[st] = sc.add("sync", lambda s, st=st, yslot=yslot:
                                      s.dma_start(
                                          d_out.ap()[st * 128:(st + 1) * 128, :],
                                          y2r[:, yslot * E:(yslot + 1) * E]),
                                      waits=[("dve", m_y2[st])],
                                      inc=(("oA", "oB")[yslot], 16))
            sc.add("sync", lambda s: s.nop(),
                   waits=[("oA", cur("oA")), ("oB", cur("oB"))])
            with nc.Block() as blk:
                sc.emit(blk, sems)

    return nc


def kernel(L0_attn_x, L0_attn_wq, L0_attn_bq, L0_attn_wk, L0_attn_bk,
           L0_attn_wv, L0_attn_bv, L0_attn_wo, L0_attn_bo,
           L0_ffn_weight, L0_ffn_disha):
    x = np.ascontiguousarray(np.asarray(L0_attn_x, dtype=np.float32))
    wq_t = np.ascontiguousarray(np.asarray(L0_attn_wq, np.float32).T)
    wk_t = np.ascontiguousarray(np.asarray(L0_attn_wk, np.float32).T)
    wv_t = np.ascontiguousarray(np.asarray(L0_attn_wv, np.float32).T)
    wo_t = np.ascontiguousarray(np.asarray(L0_attn_wo, np.float32).T)
    wf_t = np.ascontiguousarray(np.asarray(L0_ffn_weight, np.float32).T)
    disha = np.asarray(L0_ffn_disha, np.float32)
    disha2 = np.ascontiguousarray(np.concatenate([disha, disha], axis=0))
    bq_r = np.ascontiguousarray(
        np.asarray(L0_attn_bq, np.float32).reshape(8, 128).T)
    bk_r = np.ascontiguousarray(
        np.asarray(L0_attn_bk, np.float32).reshape(8, 128).T)
    bv = np.ascontiguousarray(np.asarray(L0_attn_bv, np.float32)[None, :])
    bo = np.ascontiguousarray(np.asarray(L0_attn_bo, np.float32)[None, :])
    cc = _host_consts()

    nc = build_nc()
    shared = dict(wq_t=wq_t, wk_t=wk_t, wv_t=wv_t, wo_t=wo_t, wf_t=wf_t,
                  disha2=disha2, bq_r=bq_r, bk_r=bk_r, bv=bv, bo=bo,
                  cos2=cc["cos2"], sin2=cc["sin2"], r2t=cc["r2t"],
                  identc=cc["ident"], tric=cc["tri"], ones2c=cc["ones2"],
                  onesrc=cc["ones_row"])
    in_maps = []
    for b in range(8):
        m = dict(shared)
        m["x_tm"] = np.ascontiguousarray(x[b])
        m["x_fm"] = np.ascontiguousarray(x[b].T)
        in_maps.append(m)

    res = run_bass_kernel_spmd(nc, in_maps, core_ids=list(range(8)))
    out = np.stack([np.asarray(res.results[b]["out"]) for b in range(8)],
                   axis=0)
    return out


# version 18
# speedup vs baseline: 1.0550x; 1.0550x over previous
import math
import numpy as np
import ml_dtypes
from contextlib import ExitStack

import concourse.bass as bass
import concourse.mybir as mybir
from concourse.bass_utils import run_bass_kernel_spmd

F32 = mybir.dt.float32
F16 = mybir.dt.float16
BF16 = mybir.dt.bfloat16
AF = mybir.ActivationFunctionType
ALU = mybir.AluOpType
AX = mybir.AxisListType

S = 2048
E = 1024
H = 16
DK = 64
NKC = 8
NST = 16
NPAIR = 8
EPS = 1e-5
DEBUG = False


def _host_consts():
    theta = 10000.0 ** (np.arange(0, DK, 2, dtype=np.float64) / DK)
    pos = np.arange(S, dtype=np.float64)
    freqs = pos[:, None] * (1.0 / theta)[None, :]
    emb = np.concatenate([freqs, freqs], axis=-1)
    cos_t = np.cos(emb).T
    sin_t = np.sin(emb).T
    cos2 = np.concatenate([cos_t, cos_t], axis=0).astype(np.float16)
    sin2 = np.concatenate([sin_t, sin_t], axis=0).astype(np.float16)

    R64 = np.zeros((DK, DK), dtype=np.float64)
    for j in range(DK // 2):
        R64[2 * j, 2 * j + 1] = -1.0
        R64[2 * j + 1, 2 * j] = 1.0
    R2 = np.zeros((128, 128), dtype=np.float64)
    R2[0:64, 0:64] = R64
    R2[64:128, 64:128] = R64
    r2t = np.ascontiguousarray(R2.T).astype(np.float16)

    ident = np.eye(128, dtype=np.float16)
    kk, qq = np.meshgrid(np.arange(128), np.arange(128), indexing="ij")
    tri = (qq >= kk).astype(ml_dtypes.bfloat16)
    ones2 = np.zeros((128, 2), dtype=np.float16)
    ones2[0:64, 0] = 1.0
    ones2[64:128, 1] = 1.0
    ones_row = np.ones((1, 128), dtype=np.float16)
    return dict(cos2=cos2, sin2=sin2, r2t=r2t, ident=ident, tri=tri,
                ones2=ones2, ones_row=ones_row)


class Script:

    def __init__(self, ctr):
        self.ctr = ctr
        self.items = []

    def add(self, eng, fn, waits=(), inc=None):
        if inc is not None:
            name, n = inc
            self.ctr[name] = self.ctr.get(name, 0) + n
        self.items.append((eng, tuple(waits), fn, inc))
        return None if inc is None else self.ctr[inc[0]]

    def emit(self, blk, sems):
        by_eng = {}
        for eng, waits, fn, inc in self.items:
            by_eng.setdefault(eng, []).append((waits, fn, inc))
        for eng_name, items in by_eng.items():
            def mk(items):
                def _f(eng):
                    last = {}
                    for waits, fn, inc in items:
                        for (sname, val) in waits:
                            if last.get(sname, -1) >= val:
                                continue
                            eng.wait_ge(sems[sname], val)
                            last[sname] = val
                        inst = fn(eng)
                        if inc is not None:
                            assert inst is not None
                            inst.then_inc(sems[inc[0]], inc[1])
                return _f
            getattr(blk, eng_name)(mk(items))


def build_nc():
    nc = bass.Bass("TRN2", target_bir_lowering=False,
                   detect_race_conditions=False)

    d_x_tm = nc.dram_tensor("x_tm", [S, E], F32, kind="ExternalInput")
    d_x_fm = nc.dram_tensor("x_fm", [E, S], F32, kind="ExternalInput")
    d_wq = nc.dram_tensor("wq_t", [E, E], F32, kind="ExternalInput")
    d_wk = nc.dram_tensor("wk_t", [E, E], F32, kind="ExternalInput")
    d_wv = nc.dram_tensor("wv_t", [E, E], F32, kind="ExternalInput")
    d_wo = nc.dram_tensor("wo_t", [E, E], F32, kind="ExternalInput")
    d_wf = nc.dram_tensor("wf_t", [E, E], F32, kind="ExternalInput")
    d_d2 = nc.dram_tensor("disha2", [128, E], F32, kind="ExternalInput")
    d_bqr = nc.dram_tensor("bq_r", [128, 8], F32, kind="ExternalInput")
    d_bkr = nc.dram_tensor("bk_r", [128, 8], F32, kind="ExternalInput")
    d_bv = nc.dram_tensor("bv", [1, E], F32, kind="ExternalInput")
    d_bo = nc.dram_tensor("bo", [1, E], F32, kind="ExternalInput")
    d_cos = nc.dram_tensor("cos2", [128, S], F16, kind="ExternalInput")
    d_sin = nc.dram_tensor("sin2", [128, S], F16, kind="ExternalInput")
    d_r2t = nc.dram_tensor("r2t", [128, 128], F16, kind="ExternalInput")
    d_ident = nc.dram_tensor("identc", [128, 128], F16, kind="ExternalInput")
    d_tri = nc.dram_tensor("tric", [128, 128], BF16, kind="ExternalInput")
    d_ones2 = nc.dram_tensor("ones2c", [128, 2], F16, kind="ExternalInput")
    d_onesr = nc.dram_tensor("onesrc", [1, 128], F16, kind="ExternalInput")
    d_out = nc.dram_tensor("out", [S, E], F32, kind="ExternalOutput")
    if DEBUG:
        d_dbg = {name: nc.dram_tensor("dbg_" + name, shape, dt,
                                      kind="ExternalOutput")
                 for name, shape, dt in (
                     ("o_sb", [128, NST * E], F16),
                     ("q16", [128, S], F16),
                     ("k16", [128, S], F16),
                     ("qr16", [128, S], F16),
                     ("kr16", [128, S], F16),
                     ("v_pair", [128, NST * 130], F16),
                     ("negM", [128, 1], F32),
                     ("l4", [128, 8], F32),
                     ("oT16", [128, NKC * S], F16),
                     ("y1", [128, NST * E], F16),
                 )}

    ctr = {}
    est = ExitStack()
    with est:
        cos16 = est.enter_context(nc.sbuf_tensor("cos16", [128, S], F16))
        sin16 = est.enter_context(nc.sbuf_tensor("sin16", [128, S], F16))
        tri16 = est.enter_context(nc.sbuf_tensor("tri16", [128, 128], BF16))
        ident16 = est.enter_context(nc.sbuf_tensor("ident16", [128, 128], F16))
        r2t16 = est.enter_context(nc.sbuf_tensor("r2t16", [128, 128], F16))
        ones2_16 = est.enter_context(nc.sbuf_tensor("ones2_16", [128, 2], F16))
        onesr16 = est.enter_context(nc.sbuf_tensor("onesr16", [1, 128], F16))
        bq_sb = est.enter_context(nc.sbuf_tensor("bq_sb", [128, 8], F32))
        bk_sb = est.enter_context(nc.sbuf_tensor("bk_sb", [128, 8], F32))
        bv16 = est.enter_context(nc.sbuf_tensor("bv16", [1, E], F16))
        bo16 = est.enter_context(nc.sbuf_tensor("bo16", [1, E], F16))
        x_tm16 = est.enter_context(nc.sbuf_tensor("x_tm16", [128, NST * E], F16))
        o_sb = est.enter_context(nc.sbuf_tensor("o_sb", [128, NST * E], F16))
        negM2 = est.enter_context(nc.sbuf_tensor("negM2", [128, 2], F32))
        nrm = est.enter_context(nc.sbuf_tensor("nrm", [128, 2], F32))
        ab16 = est.enter_context(nc.sbuf_tensor("ab16", [128, 1], F16))
        scr32 = est.enter_context(nc.sbuf_tensor("scr32", [32, 32], F32))
        scr32t = est.enter_context(nc.sbuf_tensor("scr32t", [32, 32], F32))
        mm1 = est.enter_context(nc.sbuf_tensor("mm1", [1, 1], F32))
        negm16 = est.enter_context(nc.sbuf_tensor("negm16", [1, 1], F16))
        l4 = est.enter_context(nc.sbuf_tensor("l4", [128, 8], F32))
        rl4 = est.enter_context(nc.sbuf_tensor("rl4", [128, 8], F32))
        bnst = est.enter_context(nc.sbuf_tensor("bnst", [128, 12], F32))
        mv = est.enter_context(nc.sbuf_tensor("mv", [128, 2], F32))
        lnv = est.enter_context(nc.sbuf_tensor("lnv", [128, 1], F32))
        rstd = est.enter_context(nc.sbuf_tensor("rstd", [128, 1], F32))
        epsb = est.enter_context(nc.sbuf_tensor("epsb", [128, 1], F32))

        ps0 = est.enter_context(nc.psum_tensor("ps0", [128, 1024], F32))
        ps1 = est.enter_context(nc.psum_tensor("ps1", [128, 1024], F32))
        pavA = est.enter_context(nc.psum_tensor("pavA", [128, 260], F32))
        pavB = est.enter_context(nc.psum_tensor("pavB", [128, 260], F32))

        sems = {}
        for name in ("te", "act", "dve", "dG", "dS", "swA", "swB", "sxm",
                     "swo", "wf", "oA", "oB", "gp"):
            sems[name] = est.enter_context(nc.semaphore("sem_" + name))

        def te(n):
            return ("te", n)

        def bump(name, n=1):
            ctr[name] = ctr.get(name, 0) + n
            return ctr[name]

        def cur(name):
            return ctr.get(name, 0)

        pair_stack = ExitStack()
        with pair_stack:
            xT16 = pair_stack.enter_context(
                nc.sbuf_tensor("xT16", [128, NKC * S], F16))
            wq16 = [pair_stack.enter_context(
                nc.sbuf_tensor(f"wq16_{i}", [128, 1024], F16)) for i in (0, 1)]
            wk16 = [pair_stack.enter_context(
                nc.sbuf_tensor(f"wk16_{i}", [128, 1024], F16)) for i in (0, 1)]
            wv16 = pair_stack.enter_context(
                nc.sbuf_tensor("wv16", [128, NKC * 512], F16))
            q16 = pair_stack.enter_context(nc.sbuf_tensor("q16", [128, S], F16))
            k16 = pair_stack.enter_context(nc.sbuf_tensor("k16", [128, S], F16))
            rqs = pair_stack.enter_context(nc.sbuf_tensor("rqs", [128, S], F16))
            qr16d = [pair_stack.enter_context(
                nc.sbuf_tensor(f"qr16_{i}", [128, S], F16)) for i in (0, 1)]
            kr16d = [pair_stack.enter_context(
                nc.sbuf_tensor(f"kr16_{i}", [128, S], F16)) for i in (0, 1)]
            v_quadd = [pair_stack.enter_context(
                nc.sbuf_tensor(f"v_quad_{i}", [128, NST * 520], F16))
                for i in (0, 1)]
            NPT = 6
            pT = [pair_stack.enter_context(
                nc.sbuf_tensor(f"pT{i}", [128, 1024], BF16))
                for i in range(NPT)]
            scrA = pair_stack.enter_context(
                nc.sbuf_tensor("scrA", [128, 260], F32))
            scrB = pair_stack.enter_context(
                nc.sbuf_tensor("scrB", [128, 260], F32))
            psq0 = pair_stack.enter_context(
                nc.psum_tensor("psq0", [128, 512], F32))
            psq1 = pair_stack.enter_context(
                nc.psum_tensor("psq1", [128, 512], F32))

            def wslice_dma(g, dst, dram, c):
                src = bass.AP(dram, c * 128,
                              [[1024, 128], [128 * 1024, 8], [1, 128]])
                dstap = bass.AP(dst, 0, [[1024, 128], [128, 8], [1, 128]])
                return g.dma_start(dstap, src)

            def wvslice_dma(g, dst, grp):
                src = bass.AP(d_wv, grp * 512,
                              [[1024, 128], [128 * 1024, 8], [1, 512]])
                dstap = bass.AP(dst, 0, [[NKC * 512, 128], [512, 8], [1, 512]])
                return g.dma_start(dstap, src)

            sc = Script(ctr)
            for dstt, dram in ((cos16, d_cos), (sin16, d_sin), (tri16, d_tri),
                               (ident16, d_ident), (r2t16, d_r2t),
                               (ones2_16, d_ones2), (onesr16, d_onesr),
                               (bq_sb, d_bqr), (bk_sb, d_bkr)):
                sc.add("sync", lambda s, a=dstt, b=dram: s.dma_start(
                    a[:, :] if a.shape[0] > 1 else a[0:1, :], b.ap()),
                    inc=("dS", 16))
            for kc in range(NKC):
                sc.add("gpsimd", lambda g, kc=kc: g.dma_start(
                    xT16[:, kc * S:(kc + 1) * S],
                    d_x_fm.ap()[kc * 128:(kc + 1) * 128, :]), inc=("dG", 16))
            xT_ready = cur("dG")
            sc.add("gpsimd", lambda g: g.dma_start(
                bv16[0:1, :], d_bv.ap()), inc=("dG", 16))
            sc.add("gpsimd", lambda g: g.dma_start(
                bo16[0:1, :], d_bo.ap()), inc=("dG", 16))
            for (ring, dram) in ((wq16, d_wq), (wk16, d_wk)):
                sc.add("gpsimd", lambda g, r=ring, d=dram: wslice_dma(
                    g, r[0], d, 0), inc=("swA", 16))
            sc.add("gpsimd", lambda g: wvslice_dma(g, wv16, 0),
                   inc=("swA", 16))
            sc.add("vector", lambda v: v.memset(scr32[:, :], 0.0))
            sc.add("vector", lambda v: v.memset(epsb[:, :], EPS))
            sc.add("sync", lambda s: s.nop(), waits=[("dS", cur("dS"))])
            sc.add("gpsimd", lambda g: g.nop(),
                   waits=[("dG", cur("dG")), ("swA", cur("swA"))])
            with nc.Block() as blk:
                sc.emit(blk, sems)

            m_expfree = {0: ("dve", 0), 1: ("dve", 0)}
            pshalf = [ps0, ps1]
            pav = [pavA, pavB]
            psq = [psq0, psq1]
            pairinfo = {}
            m_evac_prev = [None]

            for stage in range(NPAIR + 1):
                sc = Script(ctr)
                cq = stage if stage < NPAIR else None
                ca = stage - 1 if stage >= 1 else None

                if cq is not None:
                    if cq + 1 < NPAIR:
                        wsem = ("swA", "swB")[(cq + 1) % 2]
                        for (ring, dram) in ((wq16, d_wq), (wk16, d_wk)):
                            sc.add("gpsimd",
                                   lambda g, r=ring, d=dram, cc=cq + 1:
                                   wslice_dma(g, r[cc % 2], d, cc),
                                   inc=(wsem, 16))
                        if cq + 1 == 4:
                            sc.add("gpsimd", lambda g: wvslice_dma(
                                g, wv16, 1), inc=(wsem, 16))
                    if cq == 3:
                        for st in range(NST):
                            sc.add("gpsimd", lambda g, st=st: g.dma_start(
                                x_tm16[:, st * E:(st + 1) * E],
                                d_x_tm.ap()[st * 128:(st + 1) * 128, :]),
                                inc=("sxm", 16))

                if cq is not None:
                    c = cq
                    slot = c % 2
                    qr16 = qr16d[c % 2]
                    kr16 = kr16d[c % 2]
                    v_quad = v_quadd[(c // 4) % 2]
                    last_ev = [("dve", 0), ("dve", 0)]
                    info = {}
                    pairinfo[c] = info

                    def proj_unit(kind, i, c=c, slot=slot, info=info,
                                  last_ev=last_ev):
                        bank = i % 2
                        waits = []
                        if last_ev[bank][1] > 0:
                            waits.append(last_ev[bank])
                        if kind == "q" and i == 0 and c >= 1:
                            if c % 2 == 0:
                                nw = 48 + 32 * (c // 2) + (16 if c >= 6
                                                           else 0)
                            else:
                                nw = 32 * ((c + 1) // 2)
                            waits.append((("swA", "swB")[c % 2], nw))
                        if kind in ("rq",) and i == 0:
                            waits.append(("act", info["q_ev3"]))
                        if kind in ("rk",) and i == 0:
                            waits.append(("act", info["k_ev3"]))

                        if kind in ("q", "k"):
                            w16 = wq16 if kind == "q" else wk16

                            def mmf(t, i=i, bank=bank, slot=slot, w16=w16):
                                for kc in range(NKC):
                                    inst = t.matmul(
                                        psq[bank][:, :],
                                        w16[slot][:, kc * 128:(kc + 1) * 128],
                                        xT16[:, kc * S + i * 512:
                                             kc * S + i * 512 + 512],
                                        start=(kc == 0), stop=(kc == NKC - 1))
                                return inst
                        else:
                            src16 = q16 if kind == "rq" else k16

                            def mmf(t, i=i, bank=bank, src16=src16):
                                return t.matmul(
                                    psq[bank][:, :], r2t16[:, :],
                                    src16[:, i * 512:(i + 1) * 512],
                                    start=True, stop=True)
                        m_ps = sc.add("tensor", mmf, waits=waits,
                                      inc=("te", 1))
                        if kind in ("q", "k"):
                            dst = q16 if kind == "q" else k16
                            bias = bq_sb if kind == "q" else bk_sb
                            m_ev = sc.add(
                                "scalar",
                                lambda s, i=i, bank=bank, dst=dst, bias=bias,
                                c=c: s.activation(
                                    dst[:, i * 512:(i + 1) * 512],
                                    psq[bank][:, :], AF.Identity,
                                    bias=bias[:, c:c + 1], scale=1.0),
                                waits=[("te", m_ps)], inc=("act", 1))
                            last_ev[bank] = ("act", m_ev)
                            if i == 3:
                                info[kind + "_ev3"] = m_ev
                        else:
                            dst = rqs
                            m_ev = sc.add(
                                "vector",
                                lambda v, i=i, bank=bank, dst=dst:
                                v.tensor_copy(
                                    dst[:, i * 512:(i + 1) * 512],
                                    psq[bank][:, :]),
                                waits=[("te", m_ps)], inc=("dve", 1))
                            last_ev[bank] = ("dve", m_ev)
                            if i == 3:
                                info[kind + "_ev3"] = m_ev

                    def rope_unit(kind, c=c, info=info, qr16=qr16,
                                  kr16=kr16):
                        src16 = q16 if kind == "q" else k16
                        dst = qr16 if kind == "q" else kr16
                        ev3 = info[("q" if kind == "q" else "k") + "_ev3"]
                        sc.add("vector", lambda v, s16=src16, dst=dst:
                               v.tensor_mul(
                                   dst[:, :], s16[:, :], cos16[:, :]),
                               waits=[("act", ev3)])
                        sc.add("vector", lambda v: v.tensor_mul(
                            rqs[:, :], rqs[:, :], sin16[:, :]))
                        info["rope_" + kind] = sc.add(
                            "vector", lambda v, dst=dst: v.tensor_add(
                                dst[:, :], dst[:, :], rqs[:, :]),
                            inc=("dve", 1))

                    def norms_unit(info=info, qr16=qr16, kr16=kr16):
                        sc.add("vector", lambda v: v.reduce_max(
                            nrm[:, 0:1], qr16[:, :], axis=AX.X,
                            apply_absolute_value=True))
                        m_nrm = sc.add("vector", lambda v: v.reduce_max(
                            nrm[:, 1:2], kr16[:, :], axis=AX.X,
                            apply_absolute_value=True), inc=("dve", 1))
                        info["ab"] = sc.add(
                            "vector", lambda v: v.tensor_mul(
                                ab16[:, 0:1], nrm[:, 0:1], nrm[:, 1:2]),
                            waits=[("dve", m_nrm)], inc=("dve", 1))

                    def v_units(lo, hi, c=c, info=info, last_ev=last_ev,
                                v_quad=v_quad):
                        grp = c // 4
                        if lo == 0:
                            sc.add("vector", lambda v: v.memset(
                                v_quad[:, :], 1.0))
                        for st in range(lo, hi):
                            bank = st % 2
                            waits = []
                            if last_ev[bank][1] > 0:
                                waits.append(last_ev[bank])

                            def vmm(t, st=st, bank=bank, grp=grp,
                                    slot=grp % 2):
                                for kc in range(NKC):
                                    t.matmul(
                                        psq[bank][:, :],
                                        xT16[:, kc * S + st * 128:
                                             kc * S + st * 128 + 128],
                                        wv16[:, kc * 512:(kc + 1) * 512],
                                        start=(kc == 0), stop=False)
                                return t.matmul(
                                    psq[bank][:, :], onesr16[0:1, :],
                                    bv16[0:1, grp * 512:(grp + 1) * 512],
                                    start=False, stop=True)
                            m_ps = sc.add("tensor", vmm, waits=waits,
                                          inc=("te", 1))
                            m_ev = sc.add(
                                "vector",
                                lambda v, st=st, bank=bank, v_quad=v_quad:
                                v.tensor_copy(
                                    bass.AP(v_quad, st * 520,
                                            [[NST * 520, 128], [130, 4],
                                             [65, 2], [1, 64]]),
                                    bass.AP(psq[bank], 0,
                                            [[512, 128], [128, 4],
                                             [64, 2], [1, 64]])),
                                waits=[("te", m_ps)], inc=("dve", 1))
                            last_ev[bank] = ("dve", m_ev)
                        if hi == NST:
                            info["vall"] = last_ev[0][1] if \
                                last_ev[0][0] == "dve" else None
                            info["vall"] = max(
                                x[1] for x in last_ev if x[0] == "dve")

                    def mchain_unit(c=c, info=info, last_ev=last_ev):
                        par = c % 2
                        waits = [("dve", info["ab"])]
                        if last_ev[0][1] > 0:
                            waits.append(last_ev[0])
                        m_pn = sc.add("tensor", lambda t: t.matmul(
                            psq0[0:2, 448:449], ones2_16[:, :], ab16[:, 0:1],
                            start=True, stop=True, skip_group_check=True),
                            waits=waits, inc=("te", 1))
                        m_c1 = sc.add("vector", lambda v: v.tensor_copy(
                            scr32[0:2, 0:1], psq0[0:2, 448:449]),
                            waits=[("te", m_pn)], inc=("dve", 1))
                        m_c2 = sc.add("vector", lambda v: v.transpose(
                            scr32t[:, :], scr32[:, :]),
                            waits=[("dve", m_c1)], inc=("dve", 1))
                        m_c3 = sc.add("vector", lambda v: v.reduce_max(
                            mm1[0:1, 0:1], scr32t[0:1, 0:2], axis=AX.X),
                            waits=[("dve", m_c2)], inc=("dve", 1))
                        m_n16 = sc.add("vector", lambda v: v.tensor_scalar_mul(
                            negm16[0:1, 0:1], mm1[0:1, 0:1], -1.02 / 8.0),
                            waits=[("dve", m_c3)], inc=("dve", 1))
                        waits = [("dve", m_n16)]
                        if last_ev[1][1] > 0:
                            waits.append(last_ev[1])
                        m_pbc = sc.add("tensor", lambda t: t.matmul(
                            psq1[:, 448:449], onesr16[0:1, :],
                            negm16[0:1, 0:1],
                            start=True, stop=True, skip_group_check=True),
                            waits=waits, inc=("te", 1))
                        info["negM"] = sc.add(
                            "vector", lambda v, par=par: v.tensor_copy(
                                negM2[:, par:par + 1], psq1[:, 448:449]),
                            waits=[("te", m_pbc)], inc=("dve", 1))

                if ca is not None:
                    cc = ca
                    ainfo = pairinfo[cc]
                    a_qr = qr16d[cc % 2]
                    a_kr = kr16d[cc % 2]
                    a_vq = v_quadd[(cc // 4) % 2]
                    avpar = {0: None, 1: None}
                    avslot = {}

                    def att_qb(qb, cc=cc, ainfo=ainfo, a_qr=a_qr,
                               a_kr=a_kr, a_vq=a_vq, avpar=avpar):
                        ktmax = 4 * (qb + 1)
                        st_meta = {}

                        def emit_sT_mm(kt):
                            par = kt % 2
                            ps = pshalf[par]
                            qlo = max(qb * 512, kt * 128)
                            W = (qb + 1) * 512 - qlo
                            rel = qlo - qb * 512
                            waits = []
                            prev = m_expfree[par]
                            if prev[1] > 0:
                                waits.append(("act", prev[1]))

                            def smm(t, ps=ps, rel=rel, W=W, kt=kt, qb=qb):
                                t.matmul(ps[:, rel:rel + W],
                                         a_kr[0:64, kt * 128:kt * 128 + 128],
                                         a_qr[0:64, qb * 512 + rel:
                                              qb * 512 + rel + W],
                                         start=True, stop=True,
                                         tile_position=(0, 0))
                                return t.matmul(
                                    ps[:, 512 + rel:512 + rel + W],
                                    a_kr[64:128, kt * 128:kt * 128 + 128],
                                    a_qr[64:128, qb * 512 + rel:
                                         qb * 512 + rel + W],
                                    start=True, stop=True,
                                    tile_position=(64, 0))
                            m_sT = sc.add("tensor", smm, waits=waits,
                                          inc=("te", 1))
                            st_meta[kt] = [m_sT, None, None, rel]

                        def emit_expmask(kt):
                            par = kt % 2
                            slot = kt % NPT
                            pt = pT[slot]
                            m_sT, _, _, rel = st_meta[kt]
                            W = (qb + 1) * 512 - (rel + qb * 512)
                            waits = [("te", m_sT)]
                            if avslot.get(slot) is not None:
                                waits.append(("te", avslot[slot]))

                            def expf(s, pt=pt, par=par, rel=rel, W=W,
                                     cc=cc):
                                return s.activation(
                                    bass.AP(pt, rel, [[1024, 128], [512, 2],
                                                      [1, W]]),
                                    bass.AP(pshalf[par], rel,
                                            [[1024, 128], [512, 2], [1, W]]),
                                    AF.Exp,
                                    bias=negM2[:, cc % 2:cc % 2 + 1],
                                    scale=0.125)
                            m_exp = sc.add("scalar", expf, waits=waits,
                                           inc=("act", 1))
                            m_expfree[par] = ("act", m_exp)
                            m_mask = None
                            if kt >= 4 * qb:
                                def maskf(v, pt=pt, rel=rel):
                                    return v.tensor_tensor(
                                        bass.AP(pt, rel,
                                                [[1024, 128], [512, 2],
                                                 [1, 128]]),
                                        bass.AP(pt, rel,
                                                [[1024, 128], [512, 2],
                                                 [1, 128]]),
                                        bass.AP(tri16, 0,
                                                [[128, 128], [0, 2],
                                                 [1, 128]]),
                                        ALU.mult)
                                m_mask = sc.add("vector", maskf,
                                                waits=[("act", m_exp)],
                                                inc=("dve", 1))
                            st_meta[kt][1] = m_exp
                            st_meta[kt][2] = m_mask

                        def emit_av(kt):
                            par = kt % 2
                            slot = kt % NPT
                            pt = pT[slot]
                            _, m_exp, m_mask, rel = st_meta[kt]
                            j0 = rel // 128
                            waits = [("act", m_exp)]
                            if m_mask is not None:
                                waits.append(("dve", m_mask))
                            if kt == 0:
                                waits.append(("dve", ainfo["vall"]))
                                if m_evac_prev[0] is not None:
                                    waits.append(("dve", m_evac_prev[0]))

                            def avf(t, pt=pt, kt=kt, qb=qb, j0=j0,
                                    ktmax=ktmax, cc=cc):
                                for head in range(2):
                                    for j in range(j0, 4):
                                        base = head * 512 + j * 128
                                        inst = t.matmul(
                                            pav[head][:, j * 65:j * 65 + 65],
                                            pt[:, base:base + 128],
                                            a_vq[:, kt * 520 +
                                                 (cc % 4) * 130 + head * 65:
                                                 kt * 520 + (cc % 4) * 130 +
                                                 head * 65 + 65],
                                            start=(kt == 0 and j == 0),
                                            stop=(kt == ktmax - 1 and
                                                  j == 3),
                                            skip_group_check=True)
                                return inst
                            m_av = sc.add("tensor", avf, waits=waits,
                                          inc=("te", 1))
                            avpar[par] = m_av
                            avslot[slot] = m_av
                            return m_av

                        m_av_last = None
                        for g0 in range(0, ktmax, 4):
                            for kt in range(g0, min(g0 + 4, ktmax)):
                                emit_sT_mm(kt)
                                emit_expmask(kt)
                            for kt in range(g0, min(g0 + 4, ktmax)):
                                m_av_last = emit_av(kt)

                        sc.add("vector", lambda v: v.tensor_copy(
                            scrA[:, :], pavA[:, :]),
                            waits=[("te", m_av_last)])
                        m_ev2 = sc.add("vector", lambda v: v.tensor_copy(
                            scrB[:, :], pavB[:, :]), inc=("dve", 1))
                        m_evac_prev[0] = m_ev2
                        sc.add("vector", lambda v: v.tensor_copy(
                            l4[:, 0:4],
                            bass.AP(scrA, 64, [[260, 128], [65, 4]])),
                            waits=[("dve", m_ev2)])
                        m_lcp = sc.add("vector", lambda v: v.tensor_copy(
                            l4[:, 4:8],
                            bass.AP(scrB, 64, [[260, 128], [65, 4]])),
                            inc=("dve", 1))
                        m_rcp = sc.add("vector", lambda v: v.reciprocal(
                            rl4[:, 0:8], l4[:, 0:8]),
                            waits=[("dve", m_lcp)], inc=("dve", 1))
                        scr = [scrA, scrB]
                        for head in range(2):
                            for j in range(4):
                                qt = 4 * qb + j
                                hh = 2 * cc + head

                                def divf(v, head=head, j=j, qt=qt, hh=hh):
                                    return v.tensor_scalar_mul(
                                        o_sb[:, qt * E + hh * 64:
                                             qt * E + hh * 64 + 64],
                                        scr[head][:, j * 65:j * 65 + 64],
                                        rl4[:, head * 4 + j:
                                            head * 4 + j + 1])
                                sc.add("vector", divf,
                                       waits=[("dve", m_rcp)] if
                                       (head == 0 and j == 0) else [])

                if cq is not None:
                    proj_unit("q", 0)
                    proj_unit("q", 1)
                if ca is not None:
                    att_qb(0)
                if cq is not None:
                    proj_unit("q", 2)
                    proj_unit("q", 3)
                    for i in range(4):
                        proj_unit("rq", i)
                    rope_unit("q")
                if ca is not None:
                    att_qb(1)
                if cq is not None:
                    for i in range(4):
                        proj_unit("k", i)
                    for i in range(4):
                        proj_unit("rk", i)
                    rope_unit("k")
                    norms_unit()
                if ca is not None:
                    att_qb(2)
                if cq is not None and cq % 4 == 0:
                    v_units(0, 8)
                if ca is not None:
                    att_qb(3)
                if cq is not None:
                    if cq % 4 == 0:
                        v_units(8, NST)
                    else:
                        pairinfo[cq]["vall"] = pairinfo[cq - 1]["vall"]
                    mchain_unit()

                with nc.Block() as blk:
                    sc.emit(blk, sems)

        post_stack = ExitStack()
        with post_stack:
            oT16 = post_stack.enter_context(
                nc.sbuf_tensor("oT16", [128, NKC * S], F16))
            woT16 = post_stack.enter_context(
                nc.sbuf_tensor("woT16", [128, NKC * E], F16))
            y1T16 = post_stack.enter_context(
                nc.sbuf_tensor("y1T16", [128, NKC * S], F16))
            y2r = post_stack.enter_context(
                nc.sbuf_tensor("y2r", [128, 2 * E], F32))
            weff16 = post_stack.enter_context(
                nc.sbuf_tensor("weff16", [128, NKC * E], F16))
            wff32 = post_stack.enter_context(
                nc.sbuf_tensor("wff32", [128, E], F32))
            disha2 = post_stack.enter_context(
                nc.sbuf_tensor("disha2s", [128, E], F32))
            ptr = [post_stack.enter_context(
                nc.psum_tensor(f"ptr{i}", [128, 512], F16)) for i in (0, 1)]

            sc = Script(ctr)
            m_d2 = sc.add("sync", lambda s: s.dma_start(
                disha2[:, :], d_d2.ap()), inc=("dS", 16))
            for oc in range(NKC):
                sc.add("gpsimd", lambda g, oc=oc: g.dma_start(
                    woT16[:, oc * E:(oc + 1) * E],
                    d_wo.ap()[oc * 128:(oc + 1) * 128, :]),
                    inc=("swo", 16))
            m_ot_ev = []
            for g in range(32):
                qt, oc4 = divmod(g, 2)
                slot = g % 2
                waits = []
                if g >= 2:
                    waits.append(("dve", m_ot_ev[g - 2]))

                def trf(t, qt=qt, oc4=oc4, slot=slot):
                    for u in range(4):
                        oc = oc4 * 4 + u
                        inst = t.transpose(
                            ptr[slot][:, u * 128:(u + 1) * 128],
                            o_sb[:, qt * E + oc * 128:
                                 qt * E + oc * 128 + 128],
                            ident16[:, :])
                    return inst
                m_ps = sc.add("tensor", trf, waits=waits, inc=("te", 1))
                m_ot_ev.append(sc.add(
                    "vector",
                    lambda v, qt=qt, oc4=oc4, slot=slot: v.tensor_copy(
                        bass.AP(oT16, (oc4 * 4) * S + qt * 128,
                                [[NKC * S, 128], [S, 4], [1, 128]]),
                        bass.AP(ptr[slot], 0, [[512, 128], [128, 4], [1, 128]])),
                    waits=[("te", m_ps)], inc=("dve", 1)))

            m_weff = None
            for fc in range(NKC):
                waits = []
                if fc >= 1:
                    waits.append(("gp", m_weff))
                m_wdma = sc.add("sync", lambda s, fc=fc: s.dma_start(
                    wff32[:, :],
                    d_wf.ap()[fc * 128:(fc + 1) * 128, :]),
                    waits=waits, inc=("wf", 16))
                m_weff = sc.add("gpsimd", lambda g, fc=fc: g.tensor_add(
                    weff16[:, fc * E:(fc + 1) * E],
                    wff32[:, :],
                    disha2[:, :]),
                    waits=[("wf", m_wdma)] + ([("dS", m_d2)] if fc == 0
                                              else []),
                    inc=("gp", 1))

            m_z1 = {}
            for st in range(NST):
                ps = [ps0, ps1][st % 2]
                waits = [("dve", m_ot_ev[st * 2 + 1])]
                if st == 0:
                    waits.append(("swo", 128))
                    waits.append(("sxm", 256))
                if st >= 2:
                    waits.append(("dve", m_z1[st - 2]))

                def aomm(t, ps=ps, st=st):
                    for oc in range(NKC):
                        t.matmul(ps[:, 0:512],
                                 oT16[:, oc * S + st * 128:
                                      oc * S + st * 128 + 128],
                                 woT16[:, oc * E:oc * E + 512],
                                 start=(oc == 0), stop=False)
                        t.matmul(ps[:, 512:1024],
                                 oT16[:, oc * S + st * 128:
                                      oc * S + st * 128 + 128],
                                 woT16[:, oc * E + 512:oc * E + 1024],
                                 start=(oc == 0), stop=False)
                    t.matmul(ps[:, 0:512], onesr16[0:1, :],
                             bo16[0:1, 0:512], start=False, stop=True)
                    return t.matmul(ps[:, 512:1024], onesr16[0:1, :],
                                    bo16[0:1, 512:1024], start=False,
                                    stop=True)
                m_ao = sc.add("tensor", aomm, waits=waits, inc=("te", 1))

                m_z1[st] = sc.add("vector", lambda v, ps=ps, st=st:
                                  v.tensor_add(
                                      x_tm16[:, st * E:(st + 1) * E],
                                      x_tm16[:, st * E:(st + 1) * E],
                                      ps[:, :]),
                                  waits=[("te", m_ao)], inc=("dve", 1))
                sc.add("vector", lambda v, st=st: v.bn_stats(
                    bnst[:, 0:6], x_tm16[:, st * E:st * E + 512]))
                m_bs = sc.add("vector", lambda v, st=st: v.bn_stats(
                    bnst[:, 6:12], x_tm16[:, st * E + 512:(st + 1) * E]),
                    inc=("dve", 1))
                m_bn = sc.add("vector", lambda v: v.bn_aggr(
                    mv[:, 0:2], bnst[:, 0:12]), waits=[("dve", m_bs)],
                    inc=("dve", 1))
                m_ln = sc.add("scalar", lambda s: s.activation(
                    lnv[:, 0:1], mv[:, 1:2], AF.Ln, bias=epsb[:, 0:1], scale=1.0),
                    waits=[("dve", m_bn)], inc=("act", 1))
                m_rstd = sc.add("scalar", lambda s: s.activation(
                    rstd[:, 0:1], lnv[:, 0:1], AF.Exp, scale=-0.5),
                    waits=[("act", m_ln)], inc=("act", 1))
                sc.add("vector", lambda v, st=st: v.tensor_scalar(
                    x_tm16[:, st * E:(st + 1) * E],
                    x_tm16[:, st * E:(st + 1) * E],
                    mv[:, 0:1], rstd[:, 0:1],
                    ALU.subtract, ALU.mult),
                    waits=[("act", m_rstd)], inc=("dve", 1))
            ln1_done = cur("dve")
            with nc.Block() as blk:
                sc.emit(blk, sems)

            sc = Script(ctr)
            m_yt_ev = []
            for g in range(32):
                qt, fc4 = divmod(g, 2)
                slot = g % 2
                waits = []
                if g >= 2:
                    waits.append(("dve", m_yt_ev[g - 2]))

                def trf(t, qt=qt, fc4=fc4, slot=slot):
                    for u in range(4):
                        fc = fc4 * 4 + u
                        inst = t.transpose(
                            ptr[slot][:, u * 128:(u + 1) * 128],
                            x_tm16[:, qt * E + fc * 128:
                                   qt * E + fc * 128 + 128],
                            ident16[:, :])
                    return inst
                m_ps = sc.add("tensor", trf, waits=waits, inc=("te", 1))
                m_yt_ev.append(sc.add(
                    "vector",
                    lambda v, qt=qt, fc4=fc4, slot=slot: v.tensor_copy(
                        bass.AP(y1T16, (fc4 * 4) * S + qt * 128,
                                [[NKC * S, 128], [S, 4], [1, 128]]),
                        bass.AP(ptr[slot], 0, [[512, 128], [128, 4], [1, 128]])),
                    waits=[("te", m_ps)], inc=("dve", 1)))

            if DEBUG:
                for name, t in (("o_sb", o_sb), ("oT16", oT16),
                                ("y1", x_tm16)):
                    sc.add("sync", lambda s, name=name, t=t: s.dma_start(
                        d_dbg[name].ap(), t[:, :]), inc=("dS", 16))
                sc.add("sync", lambda s: s.nop(), waits=[("dS", cur("dS"))])
            m_y2 = {}
            m_outdma = {}
            for st in range(NST):
                ps = [ps0, ps1][st % 2]
                waits = [("dve", m_yt_ev[st * 2 + 1])]
                if st == 0:
                    waits.append(("gp", m_weff))
                if st >= 2:
                    waits.append(("dve", m_y2[st - 2]))

                def ffmm(t, ps=ps, st=st):
                    for fc in range(NKC):
                        t.matmul(ps[:, 0:512],
                                 y1T16[:, fc * S + st * 128:
                                       fc * S + st * 128 + 128],
                                 weff16[:, fc * E:fc * E + 512],
                                 start=(fc == 0), stop=(fc == NKC - 1))
                        inst = t.matmul(ps[:, 512:1024],
                                        y1T16[:, fc * S + st * 128:
                                              fc * S + st * 128 + 128],
                                        weff16[:, fc * E + 512:fc * E + 1024],
                                        start=(fc == 0), stop=(fc == NKC - 1))
                    return inst
                m_ff = sc.add("tensor", ffmm, waits=waits, inc=("te", 1))

                yslot = st % 2
                zwaits = [("te", m_ff)]
                if st >= 2:
                    zwaits.append((("oA", "oB")[yslot], m_outdma[st - 2]))
                sc.add("vector", lambda v, ps=ps, st=st, yslot=yslot:
                       v.tensor_add(
                           y2r[:, yslot * E:(yslot + 1) * E],
                           x_tm16[:, st * E:(st + 1) * E],
                           ps[:, :]),
                       waits=zwaits)
                sc.add("vector", lambda v, yslot=yslot: v.bn_stats(
                    bnst[:, 0:6], y2r[:, yslot * E:yslot * E + 512]))
                m_bs2 = sc.add("vector", lambda v, yslot=yslot: v.bn_stats(
                    bnst[:, 6:12], y2r[:, yslot * E + 512:(yslot + 1) * E]),
                    inc=("dve", 1))
                m_bn = sc.add("vector", lambda v: v.bn_aggr(
                    mv[:, 0:2], bnst[:, 0:12]), waits=[("dve", m_bs2)],
                    inc=("dve", 1))
                m_ln = sc.add("scalar", lambda s: s.activation(
                    lnv[:, 0:1], mv[:, 1:2], AF.Ln, bias=epsb[:, 0:1], scale=1.0),
                    waits=[("dve", m_bn)], inc=("act", 1))
                m_rstd = sc.add("scalar", lambda s: s.activation(
                    rstd[:, 0:1], lnv[:, 0:1], AF.Exp, scale=-0.5),
                    waits=[("act", m_ln)], inc=("act", 1))
                m_y2[st] = sc.add("vector", lambda v, yslot=yslot:
                                  v.tensor_scalar(
                                      y2r[:, yslot * E:(yslot + 1) * E],
                                      y2r[:, yslot * E:(yslot + 1) * E],
                                      mv[:, 0:1], rstd[:, 0:1],
                                      ALU.subtract, ALU.mult),
                                  waits=[("act", m_rstd)], inc=("dve", 1))
                m_outdma[st] = sc.add("sync", lambda s, st=st, yslot=yslot:
                                      s.dma_start(
                                          d_out.ap()[st * 128:(st + 1) * 128, :],
                                          y2r[:, yslot * E:(yslot + 1) * E]),
                                      waits=[("dve", m_y2[st])],
                                      inc=(("oA", "oB")[yslot], 16))
            sc.add("sync", lambda s: s.nop(),
                   waits=[("oA", cur("oA")), ("oB", cur("oB"))])
            with nc.Block() as blk:
                sc.emit(blk, sems)

    return nc


def kernel(L0_attn_x, L0_attn_wq, L0_attn_bq, L0_attn_wk, L0_attn_bk,
           L0_attn_wv, L0_attn_bv, L0_attn_wo, L0_attn_bo,
           L0_ffn_weight, L0_ffn_disha):
    x = np.ascontiguousarray(np.asarray(L0_attn_x, dtype=np.float32))
    wq_t = np.ascontiguousarray(np.asarray(L0_attn_wq, np.float32).T)
    wk_t = np.ascontiguousarray(np.asarray(L0_attn_wk, np.float32).T)
    wv_t = np.ascontiguousarray(np.asarray(L0_attn_wv, np.float32).T)
    wo_t = np.ascontiguousarray(np.asarray(L0_attn_wo, np.float32).T)
    wf_t = np.ascontiguousarray(np.asarray(L0_ffn_weight, np.float32).T)
    disha = np.asarray(L0_ffn_disha, np.float32)
    disha2 = np.ascontiguousarray(np.concatenate([disha, disha], axis=0))
    bq_r = np.ascontiguousarray(
        np.asarray(L0_attn_bq, np.float32).reshape(8, 128).T)
    bk_r = np.ascontiguousarray(
        np.asarray(L0_attn_bk, np.float32).reshape(8, 128).T)
    bv = np.ascontiguousarray(np.asarray(L0_attn_bv, np.float32)[None, :])
    bo = np.ascontiguousarray(np.asarray(L0_attn_bo, np.float32)[None, :])
    cc = _host_consts()

    nc = build_nc()
    shared = dict(wq_t=wq_t, wk_t=wk_t, wv_t=wv_t, wo_t=wo_t, wf_t=wf_t,
                  disha2=disha2, bq_r=bq_r, bk_r=bk_r, bv=bv, bo=bo,
                  cos2=cc["cos2"], sin2=cc["sin2"], r2t=cc["r2t"],
                  identc=cc["ident"], tric=cc["tri"], ones2c=cc["ones2"],
                  onesrc=cc["ones_row"])
    in_maps = []
    for b in range(8):
        m = dict(shared)
        m["x_tm"] = np.ascontiguousarray(x[b])
        m["x_fm"] = np.ascontiguousarray(x[b].T)
        in_maps.append(m)

    res = run_bass_kernel_spmd(nc, in_maps, core_ids=list(range(8)))
    out = np.stack([np.asarray(res.results[b]["out"]) for b in range(8)],
                   axis=0)
    return out
